# version 09 (final)
# baseline (speedup 1.0000x reference)
import numpy as np
import ml_dtypes
from contextlib import ExitStack

import concourse.bass as bass
import concourse.mybir as mybir
from concourse.bass_utils import run_bass_kernel_spmd

F32 = mybir.dt.float32
BF16 = mybir.dt.bfloat16
U8 = mybir.dt.uint8
AF = mybir.ActivationFunctionType
ALU = mybir.AluOpType
AX = mybir.AxisListType
ts = bass.ts

B, S, H = 16, 2048, 2048
NCORES = 8
BPC = B // NCORES
TPC = BPC * S
NH, HD = 16, 128
INTER = 8192
FC = H // 128
IC = INTER // 128
CSC = float(1.0 / np.sqrt(128.0))
C1 = float(1.0 / (np.sqrt(2048.0) * 64.0))
EPS = 1e-5

PAGE = 4096
NPAGES = 40

ENGINES = ("tensor", "vector", "scalar", "gpsimd", "sync")
QUEUES = ("sync", "scalar", "gpsimd")


class Buf:
    __slots__ = ("last_w", "readers")

    def __init__(self):
        self.last_w = None
        self.readers = []


class Sched:
    def __init__(self, nc, st):
        self.nc = nc
        self.ops = {e: [] for e in ENGINES}
        self.esem = {e: st.enter_context(nc.semaphore(f"es_{e}")) for e in ENGINES}
        self.qsem = {q: st.enter_context(nc.semaphore(f"qs_{q}")) for q in QUEUES}
        self.groups = {}
        self.eidx = {e: 0 for e in ENGINES}
        self.qidx = {q: 0 for q in QUEUES}
        self.waited = {e: {} for e in ENGINES}

    def _deps(self, reads, writes):
        toks = set()
        for b in reads:
            if b.last_w:
                toks.add(b.last_w)
        for b in writes:
            if b.last_w:
                toks.add(b.last_w)
            toks.update(b.readers)
        return toks

    DEDUP = True

    def _filter(self, engine, toks, strict=()):
        w = self.waited[engine]
        out = []
        for kind, src, val in set(toks) | set(strict):
            if kind == "e" and src == engine and (kind, src, val) not in strict:
                continue
            key = (kind, src)
            if self.DEDUP and w.get(key, 0) >= val:
                continue
            if val > w.get(key, 0):
                w[key] = val
            out.append((kind, src, val))
        return out

    def _commit(self, reads, writes, tok):
        for b in reads:
            b.readers.append(tok)
        for b in writes:
            b.last_w = tok
            b.readers = []

    def op(self, engine, emit, reads=(), writes=(), sideband=()):
        strict = set()
        for b in sideband:
            if b.last_w:
                strict.add(b.last_w)
        reads = list(reads) + list(sideband)
        waits = self._filter(engine, self._deps(reads, writes), strict)
        self.eidx[engine] += 1
        tok = ("e", engine, self.eidx[engine])
        self.ops[engine].append((waits, emit, ("e", engine)))
        self._commit(reads, writes, tok)
        return tok

    def add_group(self, name, size, st):
        self.groups[name] = [st.enter_context(self.nc.semaphore(f"g_{name}")), 16 * size, 0]

    def slot_batch(self, name, n):
        g = self.groups[name]
        g[1] = g[2] + 16 * n

    def dma(self, q, out_ap, in_ap, reads=(), writes=(), group=None, **kw):
        deps = self._deps(reads, writes)
        if group is not None:
            deps = {t for t in deps if not (t[0] == "g" and t[1] == group)}
        waits = self._filter(q, deps)
        if group is None:
            self.qidx[q] += 1
            tok_w = tok_r = ("q", q, 16 * self.qidx[q])
            inc = ("q", q)
        else:
            g = self.groups[group]
            g[2] += 16
            tok_w = ("g", group, g[1])
            tok_r = ("g", group, g[2])
            inc = ("g", group)

        def emit(eng, o=out_ap, i=in_ap, kw=kw):
            return eng.dma_start(o, i, **kw)

        self.ops[q].append((waits, emit, inc))
        self._commit(reads, (), tok_r)
        self._commit((), writes, tok_w)
        return tok_w

    def pin_group(self, name, bufs):
        tok = ("g", name, self.groups[name][2])
        for b in bufs:
            b.readers.append(tok)

    def barrier(self, engine, toks):
        waits = self._filter(engine, set(toks))
        if waits:
            self.ops[engine].append((waits, None, None))

    def emit_all(self, block):
        sched = self

        def mk(ename):
            def body(eng):
                for waits, emit, inc in sched.ops[ename]:
                    for kind, src, val in waits:
                        sem = (sched.esem[src] if kind == "e"
                               else sched.qsem[src] if kind == "q"
                               else sched.groups[src][0])
                        eng.wait_ge(sem, val)
                    if emit is not None:
                        ins = emit(eng)
                        if inc[0] == "e":
                            ins.then_inc(sched.esem[ename], 1)
                        elif inc[0] == "q":
                            ins.then_inc(sched.qsem[inc[1]], 16)
                        else:
                            ins.then_inc(sched.groups[inc[1]][0], 16)
            return body

        block.tensor(mk("tensor"))
        block.vector(mk("vector"))
        block.scalar(mk("scalar"))
        block.gpsimd(mk("gpsimd"))
        block.sync(mk("sync"))


class Region:
    def __init__(self, arena, page0, npages, bufs):
        self.t = arena
        self.page0 = page0
        self.npages = npages
        self.bufs = bufs

    def ap(self, dtype, rows, cols, byte_off=0, col0=0):
        esz = 2 if dtype == BF16 else 4
        lo = self.page0 * PAGE + byte_off
        v = self.t[0:rows, lo:lo + (col0 + cols) * esz].bitcast(dtype)
        if col0:
            v = v[:, col0:col0 + cols]
        return v


def _hadamard(n):
    i = np.arange(n)
    a = i[:, None] & i[None, :]
    pc = np.zeros_like(a)
    x = a.copy()
    while x.any():
        pc ^= (x & 1)
        x >>= 1
    return (1.0 - 2.0 * pc).astype(np.float32)


def make_consts():
    bf = ml_dtypes.bfloat16
    h128 = _hadamard(128).astype(bf)
    onesb = np.ones((128, 128), dtype=np.float32).astype(bf)
    onesf = np.ones((128, 128), dtype=np.float32)
    kpp = np.arange(128)[:, None]
    qp = np.arange(512)[None, :]
    maskB = np.concatenate([(qp >= 128 * a + kpp).astype(np.float32) for a in range(4)], axis=1).astype(bf)
    epsc = np.full((128, 1), EPS, dtype=np.float32)
    return {"h128": np.ascontiguousarray(h128), "onesb": onesb, "onesf": onesf,
            "maskB": np.ascontiguousarray(maskB), "epsc": epsc}


def build(tap=None):
    nc = bass.Bass("TRN2", target_bir_lowering=False)

    hs = nc.dram_tensor("hs", [TPC, H], F32, kind="ExternalInput").ap()
    wqkv = nc.dram_tensor("wqkv", [3 * H, H], F32, kind="ExternalInput").ap()
    wo = nc.dram_tensor("wo", [H, H], F32, kind="ExternalInput").ap()
    wup = nc.dram_tensor("wup", [INTER, H], F32, kind="ExternalInput").ap()
    wgate = nc.dram_tensor("wgate", [INTER, H], F32, kind="ExternalInput").ap()
    wdn = nc.dram_tensor("wdn", [H, INTER], F32, kind="ExternalInput").ap()
    in_suq = nc.dram_tensor("suq", [128, 16], F32, kind="ExternalInput").ap()
    in_sudp = nc.dram_tensor("sudp", [128, 64], F32, kind="ExternalInput").ap()
    in_wscup = nc.dram_tensor("wscup", [128, 64], F32, kind="ExternalInput").ap()
    in_wscg = nc.dram_tensor("wscg", [128, 64], F32, kind="ExternalInput").ap()
    in_wscdp = nc.dram_tensor("wscdp", [128, 16], F32, kind="ExternalInput").ap()
    in_h128 = nc.dram_tensor("h128", [128, 128], BF16, kind="ExternalInput").ap()
    in_onesb = nc.dram_tensor("onesb", [128, 128], BF16, kind="ExternalInput").ap()
    in_onesf = nc.dram_tensor("onesf", [128, 128], F32, kind="ExternalInput").ap()
    in_maskB = nc.dram_tensor("maskB", [128, 2048], BF16, kind="ExternalInput").ap()
    in_epsc = nc.dram_tensor("epsc", [128, 1], F32, kind="ExternalInput").ap()

    out = nc.dram_tensor("out", [TPC, H], F32, kind="ExternalOutput").ap()

    s_x = nc.dram_tensor("s_x", [TPC, H], BF16, kind="Internal").ap()
    s_wqkv = nc.dram_tensor("s_wqkv", [3 * H, H], BF16, kind="Internal").ap()
    s_wo = nc.dram_tensor("s_wo", [H, H], BF16, kind="Internal").ap()
    s_wup = nc.dram_tensor("s_wup", [INTER, H], BF16, kind="Internal").ap()
    s_wgate = nc.dram_tensor("s_wgate", [INTER, H], BF16, kind="Internal").ap()
    s_wdn_nat = nc.dram_tensor("s_wdn_nat", [H, INTER], BF16, kind="Internal").ap()
    s_wdnT = nc.dram_tensor("s_wdnT", [4 * IC, 128, 512], BF16, kind="Internal").ap()
    s_q = nc.dram_tensor("s_q", [BPC, NH, 128, S], BF16, kind="Internal").ap()
    s_k = nc.dram_tensor("s_k", [BPC, NH, 128, S], BF16, kind="Internal").ap()
    s_v = nc.dram_tensor("s_v", [BPC, NH, 128, S], BF16, kind="Internal").ap()
    s_attnT = nc.dram_tensor("s_attnT", [BPC, NH, 128, S], BF16, kind="Internal").ap()
    s_ln1 = nc.dram_tensor("s_ln1", [TPC, H], F32, kind="Internal").ap()
    s_ln1b = nc.dram_tensor("s_ln1b", [TPC, H], BF16, kind="Internal").ap()
    s_x2 = nc.dram_tensor("s_x2", [IC, 128, TPC], BF16, kind="Internal").ap()

    taps = {}
    if tap == "qkv":
        taps["t_q"] = (nc.dram_tensor("t_q", [BPC, NH, 128, S], BF16, kind="ExternalOutput").ap(), s_q)
        taps["t_k"] = (nc.dram_tensor("t_k", [BPC, NH, 128, S], BF16, kind="ExternalOutput").ap(), s_k)
        taps["t_v"] = (nc.dram_tensor("t_v", [BPC, NH, 128, S], BF16, kind="ExternalOutput").ap(), s_v)
    elif tap == "attnT":
        taps["t_attnT"] = (nc.dram_tensor("t_attnT", [BPC, NH, 128, S], BF16, kind="ExternalOutput").ap(), s_attnT)
    elif tap == "ln1":
        taps["t_ln1"] = (nc.dram_tensor("t_ln1", [TPC, H], F32, kind="ExternalOutput").ap(), s_ln1)
    elif tap == "x2":
        taps["t_x2"] = (nc.dram_tensor("t_x2", [IC, 128, TPC], BF16, kind="ExternalOutput").ap(), s_x2)
    elif tap == "wdnT":
        taps["t_wdnT"] = (nc.dram_tensor("t_wdnT", [4 * IC, 128, 512], BF16, kind="ExternalOutput").ap(), s_wdnT)

    with ExitStack() as stk:
        e = stk.enter_context
        c_h128 = e(nc.sbuf_tensor("c_h128", [128, 128], BF16))
        c_onesb = e(nc.sbuf_tensor("c_onesb", [128, 128], BF16))
        c_onesf = e(nc.sbuf_tensor("c_onesf", [128, 128], F32))
        c_mB = e(nc.sbuf_tensor("c_mB", [128, 2048], BF16))
        c_eps = e(nc.sbuf_tensor("c_eps", [128, 1], F32))
        c_suq = e(nc.sbuf_tensor("c_suq", [128, 16], F32))
        c_cu = e(nc.sbuf_tensor("c_cu", [128, 64], F32))
        c_cg = e(nc.sbuf_tensor("c_cg", [128, 64], F32))
        c_cdn = e(nc.sbuf_tensor("c_cdn", [128, 16], F32))
        v_sudp = e(nc.sbuf_tensor("v_sudp", [128, 64], F32))
        v_wscup = e(nc.sbuf_tensor("v_wscup", [128, 64], F32))
        v_wscg = e(nc.sbuf_tensor("v_wscg", [128, 64], F32))
        v_wscdp = e(nc.sbuf_tensor("v_wscdp", [128, 16], F32))
        mc_all = e(nc.sbuf_tensor("mc_all", [128, 2048], BF16))
        sm = e(nc.sbuf_tensor("sm", [128, 64], F32))
        rrow = e(nc.sbuf_tensor("rrow", [1, 1024], F32))
        rbt = e(nc.sbuf_tensor("rbt", [128, 1024], F32))
        p4wt = [e(nc.sbuf_tensor(f"p4wt{i}", [128, 512], BF16, is_transpose_dest=True)) for i in range(2)]
        p4tmp = e(nc.sbuf_tensor("p4tmp", [128, 512], BF16))
        arena = e(nc.sbuf_tensor("arena", [128, NPAGES * PAGE], U8, is_transpose_dest=True))
        psum = [e(nc.psum_tensor(f"ps{i}", [128, 512], F32)) for i in range(8)]

        sch = Sched(nc, stk)
        sch.add_group("consts", 10, stk)
        sch.add_group("xcast", 2, stk)
        sch.add_group("wqkv", 1, stk)
        sch.add_group("wo", 1, stk)
        sch.add_group("wup", 1, stk)
        sch.add_group("wgate", 1, stk)
        sch.add_group("wdnnat0", 0, stk)
        sch.add_group("wdnnat1", 0, stk)
        for os_ in range(4):
            sch.add_group(f"wdnT{os_}", 0, stk)
        for s_ in range(BPC):
            sch.add_group(f"qkva{s_}", 0, stk)
            sch.add_group(f"qkvb{s_}", 0, stk)
            sch.add_group(f"attnTa{s_}", 0, stk)
            sch.add_group(f"attnTb{s_}", 0, stk)
            sch.add_group(f"ln1a{s_}", 0, stk)
            sch.add_group(f"ln1b_{s_}", 0, stk)
        for th_ in range(2):
            sch.add_group(f"ln1b{th_}", 1, stk)
            sch.add_group(f"x2a{th_}", 0, stk)
            sch.add_group(f"x2b{th_}", 0, stk)
        for stt_ in range(4):
            sch.add_group(f"out{stt_}", 8, stk)
        for nm_ in ("xt", "slwo", "aw0", "aw1", "asl0", "asl1", "xr0", "xr1",
                    "lnT0", "lnT1", "dwu0", "dwu1", "dwg0", "dwg1", "p4w0", "p4w1",
                    "p4f0", "p4f1",
                    "ewt0", "ewt1", "ewt2", "ewt3", "ex20", "ex21", "ex22", "ex23",
                    "elr0", "elr1",
                    "bq0", "bk0", "bv0", "bq1", "bk1", "bv1"):
            sch.add_group(nm_, 0, stk)
        pg = [Buf() for _ in range(NPAGES)]
        pb = {k: Buf() for k in ("c_h128", "c_onesb", "c_onesf",
                                 "c_mB", "c_eps", "c_suq", "c_cu", "c_cg", "c_cdn",
                                 "v_sudp", "v_wscup", "v_wscg", "v_wscdp", "mc_all")}
        psb = [Buf() for _ in range(8)]
        smB = [Buf() for _ in range(64)]
        rrowB = [Buf(), Buf()]
        rbB = [Buf(), Buf()]
        p4wB = [Buf(), Buf()]
        p4tB = Buf()
        dX = [Buf() for _ in range(BPC)]
        dWqkv, dWo, dWup, dWgate = Buf(), Buf(), Buf(), Buf()
        dWdnNat = [Buf() for _ in range(16)]
        dWdnT = [Buf() for _ in range(4)]
        dQ = [[Buf() for _ in range(NH)] for _ in range(BPC)]
        dK = [[Buf() for _ in range(NH)] for _ in range(BPC)]
        dV = [[Buf() for _ in range(NH)] for _ in range(BPC)]
        dAttnT = [[Buf() for _ in range(NH)] for _ in range(BPC)]
        dLn1 = [Buf() for _ in range(32)]
        dLn1b = [Buf() for _ in range(2)]
        dX2 = [[Buf() for _ in range(IC)] for _ in range(2)]

        def R(p0, n):
            return Region(arena, p0, n, [pg[p] for p in range(p0, p0 + n)])

        RSQ2048 = float(1.0 / np.sqrt(2048.0))

        def LN(cidx, z, zq, dst):
            zap = z.ap(F32, 128, 2048)
            zqap = zq.ap(F32, 128, 2048)
            dap = dst.ap(F32, 128, 2048)
            cols = [sm[:, cidx + i:cidx + i + 1] for i in range(5)]
            mean, es2, var, rstd, mr = cols
            bm, b2, bv, br, bmr = (smB[cidx + i] for i in range(5))
            sch.op("scalar",
                   lambda s_, zqap=zqap, zap=zap, mean=mean:
                       s_.activation(zqap, zap, AF.Copy, scale=1.0 / 2048.0, accum_out=mean),
                   reads=z.bufs, writes=zq.bufs + [bm])
            sch.op("scalar",
                   lambda s_, zqap=zqap, zap=zap, es2=es2:
                       s_.activation(zqap, zap, AF.Square, scale=RSQ2048, accum_out=es2),
                   reads=z.bufs, writes=zq.bufs + [b2])
            sch.op("scalar", lambda s_, var=var, mean=mean: s_.activation(var, mean, AF.Square),
                   writes=[bv], sideband=[bm])
            sch.op("scalar", lambda s_, var=var: s_.activation(var, var, AF.Copy, scale=-1.0),
                   reads=[bv], writes=[bv])
            sch.op("scalar",
                   lambda s_, var=var, es2=es2: s_.activation(var, es2, AF.Identity, bias=var),
                   writes=[bv], sideband=[bv, b2])
            sch.op("scalar", lambda s_, var=var: s_.activation(var, var, AF.Ln, bias=c_eps[:, 0:1]),
                   reads=[bv, pb["c_eps"]], writes=[bv])
            sch.op("scalar", lambda s_, rstd=rstd, var=var: s_.activation(rstd, var, AF.Exp, scale=-0.5),
                   reads=[bv], writes=[br])
            sch.op("scalar", lambda s_, mr=mr, mean=mean, rstd=rstd:
                       s_.activation(mr, mean, AF.Copy, scale=rstd),
                   writes=[bmr], sideband=[br, bm])
            sch.op("scalar", lambda s_, mr=mr: s_.activation(mr, mr, AF.Copy, scale=-1.0),
                   reads=[bmr], writes=[bmr])
            sch.op("scalar",
                   lambda s_, dap=dap, zap=zap, rstd=rstd, mr=mr:
                       s_.activation(dap, zap, AF.Identity, scale=rstd, bias=mr),
                   reads=z.bufs, writes=dst.bufs, sideband=[br, bmr])

        for name, sbuf, ap_ in (("c_h128", c_h128, in_h128),
                                ("c_onesb", c_onesb, in_onesb),
                                ("c_onesf", c_onesf, in_onesf),
                                ("c_mB", c_mB, in_maskB), ("c_eps", c_eps, in_epsc),
                                ("c_suq", c_suq, in_suq), ("v_sudp", v_sudp, in_sudp),
                                ("v_wscup", v_wscup, in_wscup), ("v_wscg", v_wscg, in_wscg),
                                ("v_wscdp", v_wscdp, in_wscdp)):
            sch.dma("scalar", sbuf[:, :], ap_, writes=[pb[name]], group="consts")

        for c in range(FC):
            sch.op("vector",
                   lambda v, c=c: v.tensor_scalar_mul(mc_all[:, ts(c, 128)], c_h128[:, :], c_suq[:, c:c + 1]),
                   reads=[pb["c_h128"]], writes=[pb["mc_all"]], sideband=[pb["c_suq"]])
        sch.op("vector",
               lambda v: v.scalar_tensor_tensor(c_cu[:, :], v_sudp[:, :], 64.0, v_wscup[:, :], ALU.mult, ALU.mult),
               reads=[pb["v_sudp"], pb["v_wscup"]], writes=[pb["c_cu"]])
        sch.op("vector", lambda v: v.tensor_scalar_mul(c_cg[:, :], v_wscg[:, :], 64.0),
               reads=[pb["v_wscg"]], writes=[pb["c_cg"]])
        sch.op("vector", lambda v: v.tensor_scalar_mul(c_cdn[:, :], v_wscdp[:, :], float(1.0 / np.sqrt(INTER))),
               reads=[pb["v_wscdp"]], writes=[pb["c_cdn"]])

        for s in range(BPC):
            sch.dma("gpsimd", s_x[ts(s, S), :], hs[ts(s, S), :], writes=[dX[s]], group="xcast")
        sch.dma("gpsimd", s_wqkv, wqkv, writes=[dWqkv], group="wqkv")

        ldf = [R(22, 2), R(24, 2)]
        ldb = [R(26, 1), R(27, 1)]
        for oc in range(16):
            for kk in range(4):
                i = (oc * 4 + kk) % 2
                fr, br = ldf[i], ldb[i]
                fap = fr.ap(F32, 128, 2048)
                bap = br.ap(BF16, 128, 2048)
                sch.slot_batch(f"p4f{i}", 1)
                sch.dma("scalar", fap, wdn[ts(oc, 128), ts(kk, 2048)], writes=fr.bufs,
                        group=f"p4f{i}")
                sch.op("vector",
                       lambda v, fap=fap, bap=bap, oc=oc: v.tensor_scalar_mul(bap, fap, c_cdn[:, oc:oc + 1]),
                       reads=fr.bufs, writes=br.bufs, sideband=[pb["c_cdn"]])
                sch.slot_batch(f"wdnnat{i}", 1)
                sch.dma("gpsimd", s_wdn_nat[ts(oc, 128), ts(kk, 2048)], bap,
                        reads=br.bufs, writes=[dWdnNat[oc]], group=f"wdnnat{i}")
        sch.pin_group("wdnnat0", [pg[26]])
        sch.pin_group("wdnnat1", [pg[27]])

        xT = [R(i, 1) for i in range(16)]
        awb = [R(16, 2), R(18, 2)]
        astg = [R(20, 1), R(21, 1)]
        for s in range(BPC):
            sch.slot_batch("xt", FC)
            for fc in range(FC):
                sch.dma("sync", xT[fc].ap(BF16, 128, 2048), s_x[ts(s, S), ts(fc, 128)],
                        reads=[dX[s]], writes=xT[fc].bufs, transpose=True, group="xt")
            x1bufs_all = [bb for f in range(FC) for bb in xT[f].bufs]
            for ob in range(24):
                w = awb[ob % 2]
                sch.slot_batch(f"aw{ob % 2}", FC)
                for f in range(FC):
                    sch.dma("sync", w.ap(BF16, 128, 256, byte_off=512 * f),
                            s_wqkv[ts(ob, 256), ts(f, 128)], reads=[dWqkv], writes=w.bufs,
                            transpose=True, group=f"aw{ob % 2}")
                for sub in range(2):
                    ochunk = 2 * ob + sub
                    if ochunk < 16:
                        dst, dbuf = s_q[s, ochunk, :, :], dQ[s][ochunk]
                    elif ochunk < 32:
                        dst, dbuf = s_k[s, ochunk - 16, :, :], dK[s][ochunk - 16]
                    else:
                        dst, dbuf = s_v[s, ochunk - 32, :, :], dV[s][ochunk - 32]
                    stgr = astg[ochunk % 2]
                    for tsub in range(4):
                        bank = (tsub % 2) + 2 * sub

                        def mmgroup(t, w=w, sub=sub, tsub=tsub, bank=bank):
                            ins = None
                            for f in range(FC):
                                ins = t.matmul(psum[bank][:, :],
                                               w.ap(BF16, 128, 128, byte_off=512 * f + 256 * sub),
                                               xT[f].ap(BF16, 128, 512, col0=512 * tsub),
                                               start=(f == 0), stop=(f == FC - 1))
                            return ins

                        sch.op("tensor", mmgroup, reads=w.bufs + x1bufs_all, writes=[psb[bank]])
                        sch.op("scalar",
                               lambda sc, stgr=stgr, tsub=tsub, bank=bank:
                                   sc.copy(stgr.ap(BF16, 128, 512, col0=512 * tsub), psum[bank][:, :]),
                               reads=[psb[bank]], writes=stgr.bufs)
                    qg = f"qkva{s}" if ochunk % 2 == 0 else f"qkvb{s}"
                    sch.slot_batch(qg, 1)
                    sch.dma("gpsimd", dst, stgr.ap(BF16, 128, 2048), reads=stgr.bufs, writes=[dbuf],
                            group=qg)
            sch.pin_group(f"qkva{s}", [pg[20]])
            sch.pin_group(f"qkvb{s}", [pg[21]])
        if tap == "qkv":
            for nm in ("t_q", "t_k", "t_v"):
                tp, src = taps[nm]
                sch.dma("gpsimd", tp, src,
                        reads=[bb for sl in (dQ, dK, dV) for row in sl for bb in row])

        sch.dma("gpsimd", s_wo, wo, writes=[dWo], group="wo")

        stripe = R(24, 16)

        def stripe_ap(fc):
            return stripe.ap(BF16, 128, 512, byte_off=1024 * fc)

        def stripe_bufs(fc):
            return [stripe.bufs[fc // 4]]

        for ostripe in range(4):
            for fc in range(IC):
                i2 = fc % 2
                wap = p4wt[i2][:, :]
                sch.slot_batch(f"p4w{i2}", 1)
                sch.dma("scalar", wap, s_wdn_nat[ts(ostripe, 512), ts(fc, 128)],
                        reads=[dWdnNat[i] for i in range(16)], writes=[p4wB[i2]], transpose=True,
                        group=f"p4w{i2}")
                bank = i2
                sch.op("tensor",
                       lambda t, wap=wap, bank=bank: t.matmul(psum[bank][:, :], c_h128[:, :], wap,
                                                              start=True, stop=True),
                       reads=[p4wB[i2], pb["c_h128"]], writes=[psb[bank]])
                sch.op("scalar",
                       lambda s_, fc=fc, bank=bank: s_.copy(stripe_ap(fc), psum[bank][:, :]),
                       reads=[psb[bank]], writes=stripe_bufs(fc))
            for sbit in (1, 2, 4, 8, 16, 32):
                for c in range(IC):
                    if c & sbit:
                        continue
                    c2 = c | sbit
                    ab = stripe_bufs(c) + stripe_bufs(c2)
                    sch.op("vector",
                           lambda v, c=c, c2=c2: v.tensor_sub(p4tmp[:, :], stripe_ap(c), stripe_ap(c2)),
                           reads=ab, writes=[p4tB])
                    sch.op("vector",
                           lambda v, c=c, c2=c2: v.tensor_add(stripe_ap(c), stripe_ap(c), stripe_ap(c2)),
                           reads=ab, writes=stripe_bufs(c))
                    sch.op("gpsimd",
                           lambda g, c2=c2: g.tensor_copy(stripe_ap(c2), p4tmp[:, :]),
                           reads=[p4tB], writes=stripe_bufs(c2))
            for fc in range(IC):
                sch.slot_batch(f"wdnT{ostripe}", 1)
                sch.dma("sync", s_wdnT[ostripe * IC + fc, :, :], stripe_ap(fc),
                        reads=stripe_bufs(fc), writes=[dWdnT[ostripe]], group=f"wdnT{ostripe}")
            sch.pin_group(f"wdnT{ostripe}", [pg[p] for p in range(24, 40)])
        if tap == "wdnT":
            tp, src = taps["t_wdnT"]
            sch.dma("gpsimd", tp, src, reads=list(dWdnT))

        sch.dma("gpsimd", s_wup, wup, writes=[dWup], group="wup")
        sch.dma("gpsimd", s_wgate, wgate, writes=[dWgate], group="wgate")

        for s in range(BPC):
            attnT = [R(0, 1), R(1, 1)]
            qkvB = [(R(2, 1), R(3, 1)), (R(4, 1), R(5, 1))]
            vt = [R(6, 1), R(7, 1)]
            ptr = [R(8, 1), R(9, 1), R(10, 1)]

            def rb_ap(gp):
                return rbt[:, ts(gp, 512)]

            for h in range(NH):
                qb, kb = qkvB[h % 2]
                for nm_, reg, srcap, db in (("bq", qb, s_q, dQ), ("bk", kb, s_k, dK)):
                    sch.slot_batch(f"{nm_}{h % 2}", 1)
                    sch.dma("scalar", reg.ap(BF16, 128, 2048), srcap[s, h, :, :],
                            reads=[db[s][h]], writes=reg.bufs, group=f"{nm_}{h % 2}")
                vtok = vt[h % 2]
                sch.slot_batch(f"bv{h % 2}", 16)
                for j in range(16):
                    sch.dma("sync", vtok.ap(BF16, 128, 128, col0=128 * j),
                            s_v[s, h, :, 128 * j:128 * (j + 1)],
                            reads=[dV[s][h]], writes=vtok.bufs, transpose=True,
                            group=f"bv{h % 2}")
                ag = f"attnTa{s}" if h % 2 == 0 else f"attnTb{s}"
                for g in range(4):
                    pvb = 4 * (g % 2)
                    lb = 5 + (g % 2)
                    gp = g % 2
                    for j in range(4 * g + 4):
                        bank = 1 + j % 3
                        ptt = ptr[j % 3]
                        pt = ptt.ap(BF16, 128, 512)
                        sch.op("tensor",
                               lambda t, qb=qb, kb=kb, j=j, g=g, bank=bank:
                                   t.matmul(psum[bank][:, :], kb.ap(BF16, 128, 128, col0=128 * j),
                                            qb.ap(BF16, 128, 512, col0=512 * g), start=True, stop=True),
                               reads=qb.bufs + kb.bufs, writes=[psb[bank]])
                        sch.op("scalar",
                               lambda sc, pt=pt, bank=bank:
                                   sc.activation(pt, psum[bank][:, :], AF.Exp, scale=CSC),
                               reads=[psb[bank]], writes=ptt.bufs)
                        if j >= 4 * g:
                            a = j - 4 * g
                            sch.op("vector",
                                   lambda v, pt=pt, a=a:
                                       v.tensor_mul(pt, pt, c_mB[:, ts(a, 512)]),
                                   reads=ptt.bufs + [pb["c_mB"]], writes=ptt.bufs)
                        sch.op("tensor",
                               lambda t, pt=pt, j=j, g=g, vtok=vtok, pvb=pvb:
                                   t.matmul(psum[pvb][:, :], vtok.ap(BF16, 128, 128, col0=128 * j),
                                            pt, start=(j == 0), stop=(j == 4 * g + 3)),
                               reads=ptt.bufs + vtok.bufs, writes=[psb[pvb]])
                        sch.op("tensor",
                               lambda t, pt=pt, j=j, g=g, lb=lb:
                                   t.matmul(psum[lb][0:1, 0:512], c_onesb[:, 0:1],
                                            pt, start=(j == 0), stop=(j == 4 * g + 3)),
                               reads=ptt.bufs + [pb["c_onesb"]], writes=[psb[lb]])
                    sch.op("vector",
                           lambda v, gp=gp, lb=lb: v.reciprocal(rrow[:, ts(gp, 512)], psum[lb][0:1, 0:512]),
                           reads=[psb[lb]], writes=[rrowB[gp]])
                    sch.op("tensor",
                           lambda t, gp=gp: t.matmul(psum[7][:, :], c_onesf[0:1, :], rrow[:, ts(gp, 512)],
                                                     start=True, stop=True),
                           reads=[rrowB[gp], pb["c_onesf"]], writes=[psb[7]])
                    sch.op("scalar", lambda sc, gp=gp: sc.copy(rb_ap(gp), psum[7][:, :]),
                           reads=[psb[7]], writes=[rbB[gp]])
                    sch.op("vector",
                           lambda v, h=h, g=g, gp=gp, pvb=pvb:
                               v.tensor_mul(attnT[h % 2].ap(BF16, 128, 512, col0=512 * g),
                                            psum[pvb][:, :], rb_ap(gp)),
                           reads=[psb[pvb], rbB[gp]], writes=attnT[h % 2].bufs)
                    sch.slot_batch(ag, 1)
                    sch.dma("gpsimd", s_attnT[s, h, :, ts(g, 512)],
                            attnT[h % 2].ap(BF16, 128, 512, col0=512 * g),
                            reads=attnT[h % 2].bufs, writes=[dAttnT[s][h]], group=ag)
            sch.pin_group(f"attnTa{s}", [pg[0]])
            sch.pin_group(f"attnTb{s}", [pg[1]])
            if tap == "attnT" and s == BPC - 1:
                tp, src = taps["t_attnT"]
                sch.dma("gpsimd", tp, src, reads=[bb for sl in dAttnT for bb in sl])

            woT = R(0, 16)
            asl = [R(16, 4), R(20, 4)]
            xres = [R(24, 2), R(26, 2)]
            zr = [R(28, 2), R(30, 2)]
            zsq = [R(32, 2), R(34, 2)]
            lnf = [R(36, 2), R(38, 2)]

            def wo_ap(d, obk):
                return woT.ap(BF16, 128, 512, byte_off=1024 * (d * 4 + obk))

            sch.slot_batch("slwo", 64)
            for d in range(16):
                for obk in range(4):
                    sch.dma("sync", wo_ap(d, obk), s_wo[ts(obk, 512), ts(d, 128)],
                            reads=[dWo], writes=[woT.bufs[(d * 4 + obk) // 4]], transpose=True,
                            group="slwo")
            for gq in range(4):
                ar = asl[gq % 2]
                sch.slot_batch(f"asl{gq % 2}", 16)
                for hh in range(16):
                    sch.dma("scalar", ar.ap(BF16, 128, 512, byte_off=1024 * hh),
                            s_attnT[s, hh, :, ts(gq, 512)],
                            reads=[dAttnT[s][hh]], writes=[ar.bufs[hh // 4]], group=f"asl{gq % 2}")
                for sub in range(4):
                    tc = 4 * gq + sub
                    xr = xres[tc % 2]
                    z = zr[tc % 2]
                    zq = zsq[tc % 2]
                    lf = lnf[tc % 2]
                    sch.slot_batch(f"xr{tc % 2}", 1)
                    sch.dma("scalar", xr.ap(F32, 128, 2048),
                            hs[s * S + 128 * tc: s * S + 128 * (tc + 1), :], writes=xr.bufs,
                            group=f"xr{tc % 2}")
                    for obk in range(4):
                        bank = obk % 2

                        def wogrp(t, ar=ar, sub=sub, obk=obk, bank=bank):
                            ins = None
                            for d in range(16):
                                ins = t.matmul(psum[bank][:, :],
                                               ar.ap(BF16, 128, 128, byte_off=1024 * d, col0=128 * sub),
                                               wo_ap(d, obk), start=(d == 0), stop=(d == 15))
                            return ins

                        sch.op("tensor", wogrp, reads=ar.bufs + woT.bufs, writes=[psb[bank]])
                        sch.op("vector",
                               lambda v, z=z, xr=xr, obk=obk, bank=bank:
                                   v.tensor_add(z.ap(F32, 128, 512, col0=512 * obk), psum[bank][:, :],
                                                xr.ap(F32, 128, 512, col0=512 * obk)),
                               reads=[psb[bank]] + xr.bufs, writes=z.bufs)
                    LN(16 + 8 * (tc % 2), z, zq, lf)
                    lg = f"ln1a{s}" if tc % 2 == 0 else f"ln1b_{s}"
                    sch.slot_batch(lg, 1)
                    sch.dma("gpsimd", s_ln1[s * S + 128 * tc: s * S + 128 * (tc + 1), :],
                            lf.ap(F32, 128, 2048), reads=lf.bufs, writes=[dLn1[16 * s + tc]],
                            group=lg)
            sch.pin_group(f"ln1a{s}", [pg[36], pg[37]])
            sch.pin_group(f"ln1b_{s}", [pg[38], pg[39]])
            sch.dma("gpsimd", s_ln1b[ts(s, 2048), :], s_ln1[ts(s, 2048), :],
                    reads=[dLn1[i] for i in range(16 * s, 16 * s + 16)], writes=[dLn1b[s]],
                    group=f"ln1b{s}")

        if tap == "ln1":
            tp, src = taps["t_ln1"]
            sch.dma("gpsimd", tp, src, reads=list(dLn1))

        for th in range(2):
            lnT = [R(0, 1), R(1, 1)]
            wsA = R(2, 8)
            wtmp = R(10, 1)
            x1 = [R(16 + i, 1) for i in range(16)]

            def ws_ap(c):
                return wsA.ap(F32, 128, 512, byte_off=2048 * c)

            def ws_bufs(c):
                return [wsA.bufs[c // 2]]

            for tb in range(4):
                for fc in range(FC):
                    w = lnT[fc % 2]
                    wap = w.ap(BF16, 128, 512)
                    r0 = 2048 * th + 512 * tb
                    sch.slot_batch(f"lnT{fc % 2}", 1)
                    sch.dma("sync", wap, s_ln1b[r0:r0 + 512, ts(fc, 128)],
                            reads=[dLn1b[th]], writes=w.bufs, transpose=True, group=f"lnT{fc % 2}")
                    bank = fc % 2
                    sch.op("tensor",
                           lambda t, wap=wap, fc=fc, bank=bank:
                               t.matmul(psum[bank][:, :], mc_all[:, ts(fc, 128)], wap, start=True, stop=True),
                           reads=w.bufs + [pb["mc_all"]], writes=[psb[bank]])
                    sch.op("scalar",
                           lambda s_, fc=fc, bank=bank:
                               s_.activation(ws_ap(fc), psum[bank][:, :], AF.Copy, scale=C1),
                           reads=[psb[bank]], writes=ws_bufs(fc))
                for sbit in (1, 2, 4):
                    for c in range(FC):
                        if c & sbit:
                            continue
                        c2 = c | sbit
                        ab = ws_bufs(c) + ws_bufs(c2)
                        sch.op("vector",
                               lambda v, c=c, c2=c2: v.tensor_sub(wtmp.ap(F32, 128, 512), ws_ap(c), ws_ap(c2)),
                               reads=ab, writes=wtmp.bufs)
                        sch.op("vector",
                               lambda v, c=c, c2=c2: v.tensor_add(ws_ap(c), ws_ap(c), ws_ap(c2)),
                               reads=ab, writes=ws_bufs(c))
                        sch.op("gpsimd",
                               lambda g, c2=c2: g.tensor_copy(ws_ap(c2), wtmp.ap(F32, 128, 512)),
                               reads=wtmp.bufs, writes=ws_bufs(c2))
                for c in range(8):
                    c2 = c | 8
                    ab = ws_bufs(c) + ws_bufs(c2)
                    sch.op("vector",
                           lambda v, c=c, c2=c2, tb=tb:
                               v.tensor_add(x1[c].ap(BF16, 128, 512, col0=512 * tb), ws_ap(c), ws_ap(c2)),
                           reads=ab, writes=x1[c].bufs)
                    sch.op("vector",
                           lambda v, c=c, c2=c2, tb=tb:
                               v.tensor_sub(x1[c2].ap(BF16, 128, 512, col0=512 * tb), ws_ap(c), ws_ap(c2)),
                           reads=ab, writes=x1[c2].bufs)

            dwu = [R(0, 2), R(2, 2)]
            dwg = [R(4, 2), R(6, 2)]
            upf = R(8, 2)
            x2s = [R(10, 1), R(11, 1)]
            x1bufs = [bb for f in range(FC) for bb in x1[f].bufs]

            def upf_ap(i):
                return upf.ap(F32, 128, 512, byte_off=2048 * i)

            wu = dwu[0]
            wg = dwg[0]
            for oc in range(IC):
                if oc % 2 == 0:
                    wu = dwu[(oc // 2) % 2]
                    wg = dwg[(oc // 2) % 2]
                    sch.slot_batch(f"dwu{(oc // 2) % 2}", FC)
                    sch.slot_batch(f"dwg{(oc // 2) % 2}", FC)
                    for f in range(FC):
                        sch.dma("sync", wu.ap(BF16, 128, 256, byte_off=512 * f),
                                s_wup[ts(oc // 2, 256), ts(f, 128)], reads=[dWup], writes=wu.bufs,
                                transpose=True, group=f"dwu{(oc // 2) % 2}")
                        sch.dma("sync", wg.ap(BF16, 128, 256, byte_off=512 * f),
                                s_wgate[ts(oc // 2, 256), ts(f, 128)], reads=[dWgate], writes=wg.bufs,
                                transpose=True, group=f"dwg{(oc // 2) % 2}")
                x2r = x2s[oc % 2]
                for tsub in range(4):
                    bu, bg = 2 * (tsub % 2), 2 * (tsub % 2) + 1
                    iu = tsub % 2

                    def upgrp(t, wu=wu, oc=oc, tsub=tsub, bu=bu):
                        ins = None
                        for f in range(FC):
                            ins = t.matmul(psum[bu][:, :],
                                           wu.ap(BF16, 128, 128, byte_off=512 * f + 256 * (oc % 2)),
                                           x1[f].ap(BF16, 128, 512, col0=512 * tsub),
                                           start=(f == 0), stop=(f == FC - 1))
                        return ins

                    def ggrp(t, wg=wg, oc=oc, tsub=tsub, bg=bg):
                        ins = None
                        for f in range(FC):
                            ins = t.matmul(psum[bg][:, :],
                                           wg.ap(BF16, 128, 128, byte_off=512 * f + 256 * (oc % 2)),
                                           x1[f].ap(BF16, 128, 512, col0=512 * tsub),
                                           start=(f == 0), stop=(f == FC - 1))
                        return ins

                    sch.op("tensor", upgrp, reads=wu.bufs + x1bufs, writes=[psb[bu]])
                    sch.op("tensor", ggrp, reads=wg.bufs + x1bufs, writes=[psb[bg]])
                    sch.op("scalar",
                           lambda s_, oc=oc, bu=bu, iu=iu:
                               s_.activation(upf_ap(iu), psum[bu][:, :], AF.Copy, scale=c_cu[:, oc:oc + 1]),
                           reads=[psb[bu], pb["c_cu"]], writes=[upf.bufs[iu // 2]])
                    sch.op("scalar",
                           lambda s_, oc=oc, bg=bg, iu=iu:
                               s_.activation(upf_ap(2 + iu), psum[bg][:, :], AF.Silu, scale=c_cg[:, oc:oc + 1]),
                           reads=[psb[bg], pb["c_cg"]], writes=[upf.bufs[(2 + iu) // 2]])
                    sch.op("vector",
                           lambda v, x2r=x2r, tsub=tsub, iu=iu:
                               v.tensor_mul(x2r.ap(BF16, 128, 512, col0=512 * tsub), upf_ap(2 + iu), upf_ap(iu)),
                           reads=[upf.bufs[iu // 2], upf.bufs[(2 + iu) // 2]], writes=x2r.bufs)
                xg = f"x2a{th}" if oc % 2 == 0 else f"x2b{th}"
                sch.slot_batch(xg, 1)
                sch.dma("gpsimd", s_x2[oc, :, ts(th, 2048)], x2r.ap(BF16, 128, 2048),
                        reads=x2r.bufs, writes=[dX2[th][oc]], group=xg)
            sch.pin_group(f"x2a{th}", [pg[10]])
            sch.pin_group(f"x2b{th}", [pg[11]])
        if tap == "x2":
            tp, src = taps["t_x2"]
            sch.dma("gpsimd", tp, src, reads=dX2[0] + dX2[1])

        ewt = [R(i, 1) for i in range(4)]
        ewtB = [ewt[i].bufs[0] for i in range(4)]

        def ewt_ap(i):
            return ewt[i].ap(BF16, 128, 512)

        ex2 = [R(4 + i, 1) for i in range(4)]
        ex2B = [ex2[i].bufs[0] for i in range(4)]

        def ex2_ap(i):
            return ex2[i].ap(BF16, 128, 1024)

        elnr = [R(28, 1), R(29, 1)]
        zh = [R(8 + 2 * i, 2) for i in range(8)]
        ezq = [R(24, 2), R(26, 2)]
        for stt in range(4):
            for ob in range(4):
                for f in range(IC):
                    i4 = f % 4
                    sch.slot_batch(f"ewt{i4}", 1)
                    sch.dma("sync", ewt_ap(i4), s_wdnT[ob * IC + f, :, :],
                            reads=[dWdnT[ob]], writes=[ewtB[i4]], group=f"ewt{i4}")
                    sch.slot_batch(f"ex2{i4}", 1)
                    sch.dma("gpsimd" if i4 % 2 == 0 else "scalar",
                            ex2_ap(i4), s_x2[f, :, ts(stt, 1024)],
                            reads=[dX2[stt // 2][f]], writes=[ex2B[i4]], group=f"ex2{i4}")

                    def dngrp(t, i4=i4, f=f):
                        ins = None
                        for tc in range(8):
                            ins = t.matmul(psum[tc][:, :], ex2_ap(i4)[:, 128 * tc:128 * (tc + 1)],
                                           ewt_ap(i4), start=(f == 0), stop=(f == IC - 1))
                        return ins

                    sch.op("tensor", dngrp, reads=[ewtB[i4], ex2B[i4]], writes=list(psb))
                for tc in range(8):
                    lr = elnr[tc % 2]
                    r0 = 1024 * stt + 128 * tc
                    sch.slot_batch(f"elr{tc % 2}", 1)
                    sch.dma("scalar", lr.ap(F32, 128, 512), s_ln1[r0:r0 + 128, ts(ob, 512)],
                            reads=[dLn1[r0 // 128]], writes=lr.bufs, group=f"elr{tc % 2}")
                    sch.op("vector",
                           lambda v, tc=tc, ob=ob, lr=lr:
                               v.tensor_add(zh[tc].ap(F32, 128, 512, col0=512 * ob), psum[tc][:, :],
                                            lr.ap(F32, 128, 512)),
                           reads=[psb[tc]] + lr.bufs, writes=zh[tc].bufs)
            for tc in range(8):
                LN(32 + 8 * (tc % 2), zh[tc], ezq[tc % 2], zh[tc])
                r0 = 1024 * stt + 128 * tc
                sch.slot_batch(f"out{stt}", 1)
                sch.dma("gpsimd", out[r0:r0 + 128, :], zh[tc].ap(F32, 128, 2048),
                        reads=zh[tc].bufs, group=f"out{stt}")
            sch.pin_group(f"out{stt}", [pg[p] for p in range(8, 24)])

        fin = {("q", q_, 16 * sch.qidx[q_]) for q_ in QUEUES if sch.qidx[q_] > 0}
        fin |= {("g", f"out{stt}", sch.groups[f"out{stt}"][2]) for stt in range(4)}
        sch.barrier("gpsimd", fin)

        block = e(nc.Block())
        sch.emit_all(block)

    return nc


_CONSTS = make_consts()


def _stage_inputs(inputs):
    f32 = np.float32
    g = lambda k: np.asarray(inputs[k])
    hs = np.ascontiguousarray(g("L0_attn_hidden_states"), dtype=f32)
    su_ug = np.asarray(g("L0_ffn_SU_ug"), dtype=f32)
    su_dp = np.asarray(g("L0_ffn_SU_dp"), dtype=f32)
    wsc_ug = np.asarray(g("L0_ffn_Wscale_ug"), dtype=f32)
    wsc_dp = np.asarray(g("L0_ffn_Wscale_dp"), dtype=f32)
    common = {
        "wqkv": np.ascontiguousarray(g("L0_attn_w_qkv"), dtype=f32),
        "wo": np.ascontiguousarray(g("L0_attn_w_o"), dtype=f32),
        "wup": np.ascontiguousarray(g("L0_ffn_W_up"), dtype=f32),
        "wgate": np.ascontiguousarray(g("L0_ffn_W_gate"), dtype=f32),
        "wdn": np.ascontiguousarray(g("L0_ffn_W_down"), dtype=f32),
        "suq": np.ascontiguousarray(su_ug.reshape(16, 128).T),
        "sudp": np.ascontiguousarray(su_dp.reshape(64, 128).T),
        "wscup": np.ascontiguousarray(wsc_ug[:INTER].reshape(64, 128).T),
        "wscg": np.ascontiguousarray(wsc_ug[INTER:].reshape(64, 128).T),
        "wscdp": np.ascontiguousarray(wsc_dp.reshape(16, 128).T),
        "h128": _CONSTS["h128"], "onesb": _CONSTS["onesb"],
        "onesf": _CONSTS["onesf"],
        "maskB": _CONSTS["maskB"], "epsc": _CONSTS["epsc"],
    }
    in_maps = []
    for c in range(NCORES):
        im = dict(common)
        im["hs"] = np.ascontiguousarray(hs[BPC * c: BPC * (c + 1)].reshape(TPC, H))
        in_maps.append(im)
    return in_maps


def kernel(**inputs):
    in_maps = _stage_inputs(inputs)
    nc = build()
    res = run_bass_kernel_spmd(nc, in_maps, core_ids=list(range(NCORES)))
    shards = [np.asarray(r["out"], dtype=np.float32).reshape(BPC, S, H) for r in res.results]
    return np.concatenate(shards, axis=0)
